# version 06 (final)
# baseline (speedup 1.0000x reference)
import numpy as np
from contextlib import ExitStack

import concourse.bass as bass
import concourse.mybir as mybir
from concourse.bass_utils import run_bass_kernel_spmd

F32 = mybir.dt.float32
AF = mybir.ActivationFunctionType
ALU = mybir.AluOpType

HD = 2048
II = 4096
NN = 128
HEADS = 64
PP = 64
N_MT = 67


_EP_OPS = {
    "matmul", "transpose", "ldweights", "activation", "memset", "memzero",
    "reciprocal", "max", "match_replace", "select", "copy_predicated",
    "tensor_copy", "tensor_tensor", "tensor_add", "tensor_sub", "tensor_mul",
    "tensor_max", "tensor_scalar", "tensor_single_scalar", "tensor_scalar_add",
    "tensor_scalar_sub", "tensor_scalar_mul", "tensor_scalar_max",
    "tensor_scalar_min", "tensor_relu", "tensor_reduce", "reduce_max",
    "reduce_sum", "scalar_tensor_tensor", "tensor_tensor_scan", "dma_start",
    "bn_stats", "bn_aggr", "pool", "pool_avg", "pool_max", "stream_shuffle",
    "max_index", "max_with_indices", "collective_compute",
}


class _EP:

    def __init__(self, e):
        self._e = e
        self.last = None
        self.last_inced = True

    def __getattr__(self, name):
        a = getattr(self._e, name)
        if name in _EP_OPS and callable(a):
            def wrap(*args, **kw):
                r = a(*args, **kw)
                self.last = r
                self.last_inced = False
                return r
            return wrap
        return a


def default_cfg():
    return dict(n_cores=8, cps=4, Tc=2048, q=128, expose=())


def _chunks(start, total, step=512):
    out, off, left = [], start, total
    while left > 0:
        n = min(step, left)
        out.append((off, n))
        off += n
        left -= n
    return out


class _StopBuild(Exception):
    pass


class MK:
    def __init__(self, cfg):
        self.cfg = cfg
        self.nc = bass.Bass("TRN2", target_bir_lowering=False,
                            detect_race_conditions=False)
        self.cps = cfg["cps"]
        self.Tc = cfg["Tc"]
        self.q = cfg["q"]
        self.Hc = self.Tc // 2
        self.NCH = self.Tc // self.q
        self.nch_h = self.Hc // self.q
        self.EW = self.Hc + 3
        self.ntt = (self.EW + 127) // 128
        self.nth = self.Hc // 128
        assert self.q == 128 and self.Hc % 128 == 0
        self.ex = cfg.get("expose", ())
        self.blk_i = 0
        self.io = {}
        self._declare_io()

    def dram(self, name, shape, kind="Internal"):
        if kind == "Internal" and name in self.ex:
            kind = "ExternalOutput"
        self.io[name] = self.nc.dram_tensor(name, shape, F32, kind=kind).ap()
        return self.io[name]

    def _declare_io(self):
        Tc, cps = self.Tc, self.cps
        d = self.dram
        d("x_sl", [Tc + 3, HD], "ExternalInput")
        d("wipT_t", [N_MT, 128, 16, 128], "ExternalInput")
        d("wopT_t", [16, 4, 128, 8, 128], "ExternalInput")
        d("w1T_t", [64, 128, 16, 128], "ExternalInput")
        d("w2T_t", [16, 8, 128, 8, 128], "ExternalInput")
        d("wrT_t", [128, 16, 16], "ExternalInput")
        d("wmodT_t", [96, 128, 16, 128], "ExternalInput")
        d("cwT2", [128, 4 * 34], "ExternalInput")
        d("convbT", [128, 34], "ExternalInput")
        d("dxmap", [128, 32], "ExternalInput")
        d("nwmap", [128, 32], "ExternalInput")
        d("c_T", [128, 16], "ExternalInput")
        d("dtb", [64, 1], "ExternalInput")
        d("alog", [64, 1], "ExternalInput")
        d("ident", [128, 128], "ExternalInput")
        d("maskUT", [128, 128], "ExternalInput")
        d("selD", [64, cps * cps], "ExternalInput")
        d("selJ", [64, cps], "ExternalInput")
        d("halo", [1, 1], "ExternalInput")
        d("br", [1, 16], "ExternalInput")
        d("epsv", [128, 1], "ExternalInput")
        d("y_out", [Tc, HD], "ExternalOutput")
        d("zn_sp", [32, 128, Tc])
        d("xbc_sp", [34, 128, Tc + 3])
        d("y_sp", [self.NCH, 128, 4096])
        d("p_sp", [self.NCH, 128, 4096])
        d("gated_sp", [32, 128, Tc])
        d("hh_sp", [64, 128, Tc])
        d("x1_sp", [16, 128, Tc])
        d("brow", [1, 4096])
        d("cc_in", [128, 4104])
        d("cc_out", [cps * 128, 4104])

    def sb(self, name, shape, stack=None, dt=F32):
        return (stack or self.ctx).enter_context(self.nc.sbuf_tensor("s_" + name, shape, dt))

    def block(self, **eng_fns):
        snap = dict(self.cnt)
        self.blk_i += 1
        with self.nc.Block(f"b{self.blk_i}") as b:
            for name, fn in eng_fns.items():
                runner = {"sync": b.sync, "tensor": b.tensor, "vector": b.vector,
                          "scalar": b.scalar, "gpsimd": b.gpsimd}[name]

                def outer(e, fn=fn, snap=snap):
                    ep = _EP(e)
                    for s, v in snap.items():
                        if v > 0:
                            e.wait_ge(self.sem[s], v)
                    fn(ep)
                runner(outer)

    def inc(self, sname, inst, n=1):
        inst.then_inc(self.sem[sname], n)
        self.cnt[sname] += n
        return self.cnt[sname]

    def seminc(self, e, sname, n=1):
        last = getattr(e, "last", None)
        if last is not None and not getattr(e, "last_inced", True):
            try:
                last.then_inc(self.sem[sname], n)
                e.last_inced = True
                self.cnt[sname] += n
                return self.cnt[sname]
            except Exception:
                pass
        e.sem_inc(self.sem[sname], n)
        self.cnt[sname] += n
        return self.cnt[sname]

    def dq(self, e, pool, out, in_, **kw):
        i = self.pidx[pool] % 6
        self.pidx[pool] += 1
        name = f"{pool}{i}"
        if self.cnt[name] > 0:
            e.wait_ge(self.sem[name], self.cnt[name])
        d = e.dma_start(out=out, in_=in_, **kw)
        d.then_inc(self.sem[name], 16)
        if hasattr(e, "last_inced"):
            e.last_inced = True
        self.cnt[name] += 16
        return (name, self.cnt[name])

    def wt(self, e, tok):
        e.wait_ge(self.sem[tok[0]], tok[1])

    def vgap(self, e, n=2):
        for _ in range(n):
            e.tensor_copy(self.gapt[:, :], self.ident[:, 0:64])

    def bc(self, e, dst, src_row, w, flat_parts=None):
        io = self.io
        if flat_parts:
            ov = io["brow"][0:1, 0:w].rearrange("p (b c) -> (p b) c", b=flat_parts)
            t1 = self.dq(e, "G", ov, src_row)
        else:
            t1 = self.dq(e, "G", io["brow"][0:1, 0:w], src_row)
        self.wt(e, t1)
        return self.dq(e, "G", dst,
                       io["brow"][0:1, 0:w].partition_broadcast(dst.shape[0]))

    def build(self):
        nc = self.nc
        with ExitStack() as ctx:
            self.ctx = ctx
            self.sem = {}
            self.cnt = {}
            names = ["ts", "vs", "ss", "cc"]
            names += [f"L{i}" for i in range(6)] + [f"G{i}" for i in range(6)]
            for s in names:
                self.sem[s] = ctx.enter_context(nc.semaphore(s))
                self.cnt[s] = 0
            self.pidx = {"L": 0, "G": 0}
            stop = self.cfg.get("stop_after", "all")
            self._alloc_persistent()
            done = False
            for stage, fn in self._stage_list():
                self._stopped = False
                fn()
                if self._stopped:
                    done = True
                    break
                if stage == stop or (stage == "h0s12" and stop in ("h0s1", "h0s2")) \
                        or (stage == "h1s12" and stop in ("h1s1", "h1s2")):
                    done = True
                    break
            self.m_finish()
        return self.nc

    def _stage_list(self):
        out = [("consts", self.m_consts), ("mod", self.m_mod)]
        for half in range(2):
            def s12(half=half):
                with ExitStack() as stA:
                    self.m_s1_s2(half, stA)

            def s3(half=half):
                with ExitStack() as stB:
                    self.m_s3(half, stB)
            out.append((f"h{half}s12", s12))
            out.append((f"h{half}s3", s3))
        out.append(("xchg", self.m_exchange))
        for half in range(2):
            def s4(half=half):
                with ExitStack() as stC:
                    self.m_s4(half, stC)

            def s5(half=half):
                with ExitStack() as stD:
                    self.m_s5(half, stD)

            def s6(half=half):
                with ExitStack() as stE:
                    self.m_s6(half, stE)

            def s7(half=half):
                with ExitStack() as stF:
                    self.m_s7(half, stF)
            out.append((f"h{half}s4", s4))
            out.append((f"h{half}s5", s5))
            out.append((f"h{half}s6", s6))
            out.append((f"h{half}s7", s7))
        return out

    def _alloc_persistent(self):
        sb = self.sb
        Tc, NCH = self.Tc, self.NCH
        self.ident = sb("ident", [128, 128])
        self.maskUT = sb("maskUT", [128, 128])
        self.cwT2 = sb("cwT2", [128, 4 * 34])
        self.convbT = sb("convbT", [128, 34])
        self.dxmap = sb("dxmap", [128, 32])
        self.nwmap = sb("nwmap", [128, 32])
        self.dtbc = sb("dtbc", [64, 1])
        self.acol = sb("acol", [64, 1])
        self.onesc = sb("onesc", [128, 1])
        self.brB = sb("brB", [128, 16])
        self.epsc = sb("epsc", [128, 1])
        self.zpad = sb("zpad", [128, 8])
        self.halocol = sb("halocol", [128, 1])
        self.saC = sb("saC", [128, 16])
        self.ca1C = sb("ca1C", [128, 16])
        self.gaC = sb("gaC", [128, 16])
        self.sbC = sb("sbC", [128, 16])
        self.cb1C = sb("cb1C", [128, 16])
        self.gbC = sb("gbC", [128, 16])
        self.dt_sb = sb("dt_sb", [64, Tc])
        self.dtraw = sb("dtraw", [64, Tc])
        self.cs_sb = sb("cs_sb", [64, Tc])
        self.tot_sb = sb("tot_sb", [64, NCH])
        self.ptot_sb = sb("ptot_sb", [64, NCH + 1])
        self.segd = sb("segd", [64, 1])
        self.C_all = sb("C_all", [128, Tc])
        self.Pst = [sb(f"Pst{i}", [128, 4096]) for i in range(2)]
        self.Msb = sb("Msb", [128, 4096])
        self.selDc = sb("selDc", [64, self.cps * self.cps])
        self.selJc = sb("selJc", [64, self.cps])
        self.rs = sb("rs", [4, 64])
        self.murow = sb("murow", [1, self.Hc])
        self.sqrow = sb("sqrow", [1, self.Hc])
        self.cc1 = sb("cc1", [64, 1])
        self.cc2 = sb("cc2", [64, 1])
        self.AHC = sb("AHC", [128, 64])
        self.maskF = sb("maskF", [16, self.Hc])
        self.sm8 = sb("sm8", [128, 8])
        self.mrow = sb("mrow", [128, 16])
        self.tmp16 = sb("tmp16", [128, 16])
        self.mrowB = sb("mrowB", [128, 16])
        self.tmp16B = sb("tmp16B", [128, 16])
        self.gapt = sb("gapt", [128, 64])
        self.acoln = sb("acoln", [64, 1])
        self.wrS = sb("wrS", [128, 256])
        self.PS = [self.ctx.enter_context(
            self.nc.psum_tensor(f"psb{i}", [128, 512], F32)) for i in range(8)]

    def m_consts(self):
        S, C, io = self.sem, self.cnt, self.io

        toks = {}

        def sy(e):
            for t, src in ((self.ident, "ident"), (self.maskUT, "maskUT"),
                           (self.cwT2, "cwT2"), (self.convbT, "convbT"),
                           (self.dxmap, "dxmap"), (self.nwmap, "nwmap"),
                           (self.selDc, "selD"), (self.selJc, "selJ"),
                           (self.dtbc, "dtb"), (self.acol, "alog"),
                           (self.epsc, "epsv")):
                toks[src] = self.dq(e, "L", t[:, :], io[src][:, :])
            toks["wr"] = self.dq(e, "L", self.wrS[:, :],
                                 io["wrT_t"].rearrange("kr kt c -> kr (kt c)"))

        def gp(e):
            self.dq(e, "G", self.brB[:, :], io["br"][0:1, :].partition_broadcast(128))
            self.dq(e, "G", self.halocol[:, :],
                    io["halo"][0:1, :].partition_broadcast(128))

        def ve(e):
            e.memset(self.onesc[:, :], 1.0)
            e.memset(self.Msb[:, :], 0.0)
            e.memset(self.ptot_sb[0:64, 0:1], 0.0)
            e.memset(self.Pst[0][:, :], 0.0)
            e.memset(self.zpad[:, :], 0.0)
            self.seminc(e, "vs")

        def sc(e):
            self.wt(e, toks["alog"])
            a = e.activation(out=self.acol[:, :], in_=self.acol[:, :], func=AF.Exp)
            self.inc("ss", a)

        self.block(sync=sy, gpsimd=gp, vector=ve, scalar=sc)

        def ve2(e):
            e.tensor_scalar_mul(self.acoln[:, :], self.acol[:, :], -1.0)
            self.seminc(e, "vs")

        self.block(vector=ve2)

    def m_mod(self):
        S, C, io, PS = self.sem, self.cnt, self.io, self.PS
        st = ExitStack()
        cT = self.sb("cT", [128, 16], stack=st)
        svT = self.sb("svT", [128, 16], stack=st)
        wmt = [self.sb(f"wmt{i}", [128, HD], stack=st) for i in range(2)]

        ts0, ss0 = C["ts"], C["ss"]
        ctok = [None]
        wtoks = []

        def sy(e):
            ctok[0] = self.dq(e, "L", cT[:, :], io["c_T"][:, :])
            for mt in range(96):
                if mt >= 2:
                    e.wait_ge(S["ts"], ts0 + (mt - 1))
                wtoks.append(self.dq(
                    e, "L", wmt[mt % 2][:, :],
                    io["wmodT_t"][mt].rearrange("kr kt mc -> kr (kt mc)")))

        def sc(e):
            self.wt(e, ctok[0])
            a = e.activation(out=svT[:, :], in_=cT[:, :], func=AF.Silu)
            self.inc("ss", a)

        def tn(e):
            e.wait_ge(S["ss"], ss0 + 1)
            for mt in range(96):
                self.wt(e, wtoks[mt])
                for kt in range(16):
                    mm = e.matmul(PS[2][0:128, mt:mt + 1],
                                  wmt[mt % 2][:, kt * 128:(kt + 1) * 128],
                                  svT[:, kt:kt + 1],
                                  start=(kt == 0), stop=(kt == 15),
                                  skip_group_check=True)
                self.inc("ts", mm)

        self.block(sync=sy, scalar=sc, tensor=tn)
        st.close()

        def vp(e):
            ga, gb = self.gaC, self.gbC
            mA, mB = self.mrow, self.mrowB
            tA, tB = self.tmp16, self.tmp16B
            order = [(ga, 2), (gb, 5), (self.saC, 0), (self.ca1C, 1),
                     (self.sbC, 3), (self.cb1C, 4)]
            for dst, vi in order:
                e.tensor_copy(dst[:, :], PS[2][:, vi * 16:(vi + 1) * 16])
            e.tensor_scalar(mA[:, :], ga[:, :], 0.0, None, ALU.is_ge)
            e.tensor_scalar(mB[:, :], gb[:, :], 0.0, None, ALU.is_ge)
            e.tensor_scalar_mul(tA[:, :], ga[:, :], -1.0)
            e.tensor_scalar_mul(tB[:, :], gb[:, :], -1.0)
            e.tensor_scalar_add(self.ca1C[:, :], self.ca1C[:, :], 1.0)
            e.tensor_scalar(mA[:, :], mA[:, :], 2.0, -1.0, ALU.mult, ALU.add)
            e.tensor_scalar(mB[:, :], mB[:, :], 2.0, -1.0, ALU.mult, ALU.add)
            e.tensor_max(ga[:, :], ga[:, :], tA[:, :])
            e.tensor_max(gb[:, :], gb[:, :], tB[:, :])
            e.tensor_scalar_add(self.cb1C[:, :], self.cb1C[:, :], 1.0)
            self.vgap(e, 1)
            e.tensor_scalar_max(ga[:, :], ga[:, :], 0.25)
            e.tensor_scalar_max(gb[:, :], gb[:, :], 0.25)
            self.vgap(e, 2)
            e.tensor_mul(ga[:, :], ga[:, :], mA[:, :])
            e.tensor_mul(gb[:, :], gb[:, :], mB[:, :])
            self.seminc(e, "vs")

        self.block(vector=vp)

    def m_s1_s2(self, half, stA):
        S, C, io, PS = self.sem, self.cnt, self.io, self.PS
        Hc, EW, ntt = self.Hc, self.EW, self.ntt
        hoff = half * Hc
        ext0 = half * Hc

        hf = [self.sb(f"hf{half}_{i}", [128, EW], stack=stA) for i in range(16)]
        xt = self.sb(f"xt{half}", [128, HD], stack=stA)
        stats = self.sb(f"stats{half}", [128, 4], stack=stA)
        wsl = [self.sb(f"wsl{half}_{i}", [128, HD], stack=stA) for i in range(2)]
        zev = [self.sb(f"zev{half}_{i}", [128, 512], stack=stA) for i in range(2)]
        zevA = self.sb(f"zevA{half}", [128, 512], stack=stA)
        spd = self.sb(f"spd{half}", [64, Hc], stack=stA)
        junk = self.sb(f"junk{half}", [128, HD], stack=stA)

        ss0, vs0, ts0 = C["ss"], C["vs"], C["ts"]
        xtoks = []

        def sy(e):
            for tt in range(ntt):
                r0 = tt * 128
                rn = min(128, EW - r0)
                if tt >= 1:
                    e.wait_ge(S["ts"], ts0 + tt * 16)
                xtoks.append(self.dq(e, "L", xt[0:rn, :],
                                     io["x_sl"][ext0 + r0:ext0 + r0 + rn, :]))

        def v_norm(tt):
            return vs0 + 20 * tt + 4

        def v_evac(i):
            return vs0 + 20 * (i // 16) + 4 + (i % 16) + 1

        def sc(e):
            for tt in range(ntt):
                rn = min(128, EW - tt * 128)
                e.wait_ge(S["vs"], vs0 + 20 * tt + 1)
                a = e.activation(out=stats[0:rn, 1:2], in_=stats[0:rn, 0:1],
                                 func=AF.Copy, scale=1.0 / HD)
                self.inc("ss", a)
                e.wait_ge(S["vs"], vs0 + 20 * tt + 2)
                a = e.activation(out=junk[0:rn, :], in_=xt[0:rn, :], func=AF.Square)
                self.inc("ss", a)
                e.wait_ge(S["vs"], vs0 + 20 * tt + 3)
                a = e.activation(out=stats[0:rn, 3:4], in_=stats[0:rn, 2:3],
                                 func=AF.Sqrt, bias=self.epsc[0:rn, 0:1],
                                 scale=1.0 / HD)
                self.inc("ss", a)

        def ve(e):
            for tt in range(ntt):
                r0 = tt * 128
                rn = min(128, EW - r0)
                self.wt(e, xtoks[tt])
                r = e.reduce_sum(stats[0:rn, 0:1], xt[0:rn, :],
                                 axis=mybir.AxisListType.X)
                self.inc("vs", r)
                e.wait_ge(S["ss"], ss0 + 3 * tt + 1)
                r = e.tensor_scalar_sub(xt[0:rn, :], xt[0:rn, :], stats[0:rn, 1:2])
                self.inc("vs", r)
                e.wait_ge(S["ss"], ss0 + 3 * tt + 2)
                r = e.reduce_sum(stats[0:rn, 2:3], junk[0:rn, :],
                                 axis=mybir.AxisListType.X)
                self.inc("vs", r)
                e.wait_ge(S["ss"], ss0 + 3 * tt + 3)
                e.reciprocal(stats[0:rn, 2:3], stats[0:rn, 3:4])
                self.vgap(e, 2)
                r = e.tensor_scalar_mul(xt[0:rn, :], xt[0:rn, :], stats[0:rn, 2:3])
                self.inc("vs", r)
                for ft in range(16):
                    i = tt * 16 + ft
                    e.wait_ge(S["ts"], ts0 + i + 1)
                    e.tensor_scalar(hf[ft][:, r0:r0 + rn], PS[4 + i % 4][:, 0:rn],
                                    self.ca1C[:, ft:ft + 1], self.saC[:, ft:ft + 1],
                                    ALU.mult, ALU.add)
                    self.seminc(e, "vs")

        def tn(e):
            for tt in range(ntt):
                r0 = tt * 128
                rn = min(128, EW - r0)
                e.wait_ge(S["vs"], v_norm(tt))
                for ft in range(16):
                    i = tt * 16 + ft
                    if i >= 4:
                        e.wait_ge(S["vs"], v_evac(i - 4))
                    mm = e.transpose(PS[4 + i % 4][:, 0:rn],
                                     xt[0:rn, ft * 128:(ft + 1) * 128],
                                     self.ident[0:rn, 0:rn])
                    self.inc("ts", mm)

        self.block(sync=sy, scalar=sc, vector=ve, tensor=tn)
        if self.cfg.get("stop_after") == f"h{half}s1":
            return

        xbc_ch = _chunks(0, EW) if half == 0 else _chunks(3, EW - 3)
        z_ch = _chunks(3, EW - 3)
        plans = [xbc_ch if 32 <= mt < 66 else z_ch for mt in range(N_MT)]
        cum = np.cumsum([0] + [len(p) for p in plans])

        tiles = []
        k_ship = 0
        for mt in range(N_MT):
            for (off, n) in plans[mt]:
                kind = "z" if mt < 32 else ("xbc" if mt < 66 else "dt")
                sk = k_ship if kind != "dt" else -1
                if kind != "dt":
                    k_ship += 1
                tiles.append((mt, off, n, kind, sk))
        n_j = len(tiles)

        ts0 = C["ts"]
        vs0 = C["vs"]
        ss0 = C["ss"]
        ss_cum = np.cumsum([0] + [1 if t[3] == "z" else 0 for t in tiles])
        slab_toks = []
        ship_toks = []

        def sy2(e):
            for mt in range(N_MT):
                if mt >= 2:
                    e.wait_ge(S["ts"], ts0 + int(cum[mt - 1]))
                slab_toks.append(self.dq(
                    e, "L", wsl[mt % 2][:, :],
                    io["wipT_t"][mt].rearrange("kr kt mc -> kr (kt mc)")))

        def tn2(e):
            for j, (mt, off, n, kind, sk) in enumerate(tiles):
                if j == int(cum[mt]):
                    self.wt(e, slab_toks[mt])
                mr = 64 if kind == "dt" else 128
                pb = PS[j % 4]
                if j >= 4:
                    e.wait_ge(S["vs"], vs0 + (j - 3))
                for kt in range(16):
                    mm = e.matmul(pb[0:mr, 0:n],
                                  wsl[mt % 2][:, kt * 128:kt * 128 + mr],
                                  hf[kt][:, off:off + n],
                                  start=(kt == 0), stop=(kt == 15))
                self.inc("ts", mm)

        def gp2(e):
            for j, (mt, off, n, kind, sk) in enumerate(tiles):
                if kind == "dt":
                    ship_toks.append(None)
                    continue
                e.wait_ge(S["vs"], vs0 + j + 1)
                if kind == "z":
                    dst = io["zn_sp"][mt, :, hoff + off - 3:hoff + off - 3 + n]
                else:
                    dst = io["xbc_sp"][mt - 32, :, ext0 + off:ext0 + off + n]
                ship_toks.append(self.dq(e, "G", dst, zev[sk % 2][0:128, 0:n]))

        shipped_j = [j for j, t in enumerate(tiles) if t[3] != "dt"]

        zjs = [j for j, t in enumerate(tiles) if t[3] == "z"]

        def sc2(e):
            for zi, j in enumerate(zjs):
                mt, off, n, kind, sk = tiles[j]
                e.wait_ge(S["ts"], ts0 + j + 1)
                if zi >= 1:
                    e.wait_ge(S["vs"], vs0 + zjs[zi - 1] + 1)
                a = e.activation(out=zevA[0:128, 0:n],
                                 in_=PS[j % 4][0:128, 0:n], func=AF.Silu)
                self.inc("ss", a)

        def ve2(e):
            for j, (mt, off, n, kind, sk) in enumerate(tiles):
                if kind == "z":
                    e.wait_ge(S["ss"], ss0 + int(ss_cum[j + 1]))
                    if sk >= 2:
                        self.wt(e, ship_toks[shipped_j[sk - 2]])
                    e.tensor_scalar_mul(zev[sk % 2][0:128, 0:n],
                                        zevA[0:128, 0:n],
                                        self.nwmap[:, mt:mt + 1])
                    self.seminc(e, "vs")
                elif kind == "xbc":
                    e.wait_ge(S["ts"], ts0 + j + 1)
                    if sk >= 2:
                        self.wt(e, ship_toks[shipped_j[sk - 2]])
                    e.tensor_copy(zev[sk % 2][0:128, 0:n], PS[j % 4][0:128, 0:n])
                    self.seminc(e, "vs")
                else:
                    e.wait_ge(S["ts"], ts0 + j + 1)
                    e.tensor_copy(self.dtraw[0:64, hoff + off - 3:hoff + off - 3 + n],
                                  PS[j % 4][0:64, 0:n])
                    self.seminc(e, "vs")

        self.block(sync=sy2, tensor=tn2, gpsimd=gp2, scalar=sc2, vector=ve2)
        if self.cfg.get("stop_after") == f"h{half}s2":
            return

        def vd(e):
            sl = self.dtraw[0:64, hoff:hoff + Hc]
            e.tensor_scalar_add(sl, sl, self.dtbc[:, 0:1])
            self.seminc(e, "vs")

        self.block(vector=vd)

        def sd(e):
            sl = self.dtraw[0:64, hoff:hoff + Hc]
            e.activation(out=spd[:, :], in_=sl, func=AF.Abs)
            e.activation(out=spd[:, :], in_=spd[:, :], func=AF.Exp, scale=-1.0)
            a = e.activation(out=spd[:, :], in_=spd[:, :], func=AF.Ln, bias=1.0)
            self.inc("ss", a)

        self.block(scalar=sd)

        def vd2(e):
            sl = self.dtraw[0:64, hoff:hoff + Hc]
            dl = self.dt_sb[0:64, hoff:hoff + Hc]
            e.scalar_tensor_tensor(dl, sl, 0.0, spd[:, :], ALU.max, ALU.add)
            e.tensor_scalar(dl, dl, 0.0, 100.0, ALU.max, ALU.min)
            self.seminc(e, "vs")

        self.block(vector=vd2)

    def m_s3(self, half, stB):
        q = self.q
        bufs = dict(
            xbp=[self.sb(f"xbp{half}_{i}", [128, 9 * 131], stack=stB) for i in range(2)],
            cvt=self.sb(f"cvt{half}", [128, 9 * 128], stack=stB),
            tmpc=self.sb(f"tmpc{half}", [128, 9 * 128], stack=stB),
            xconv=self.sb(f"xconv{half}", [128, 34 * 128], stack=stB),
            Xt=self.sb(f"Xts{half}", [128, 4096], stack=stB),
            BT=self.sb(f"BTs{half}", [128, 128], stack=stB),
            csT=self.sb(f"csTs{half}", [128, 64], stack=stB),
            dtT=self.sb(f"dtTs{half}", [128, 64], stack=stB),
            dtwT=self.sb(f"dtwT{half}", [128, 64], stack=stB),
            Gm=self.sb(f"Gms{half}", [128, 128], stack=stB),
            Esp=self.sb(f"Esps{half}", [128, 512], stack=stB),
            EspX=self.sb(f"EspXs{half}", [128, 512], stack=stB),
            MT2=self.sb(f"MT2s{half}", [128, 512], stack=stB),
            Xp=[self.sb(f"Xps{half}_{i}", [128, 256], stack=stB) for i in range(2)],
            Xw=[self.sb(f"Xws{half}_{i}", [128, 256], stack=stB) for i in range(2)],
            XDp=self.sb(f"XDps{half}", [128, 512], stack=stB),
            Ysb=self.sb(f"Ysbs{half}", [128, 4096], stack=stB),
            csb4=[self.sb(f"csb4s{half}_{i}", [128, 2048], stack=stB) for i in range(2)],
            adt=self.sb(f"adts{half}", [64, 128], stack=stB),
            wch=self.sb(f"wchs{half}", [64, 128], stack=stB),
            wche=self.sb(f"wches{half}", [64, 128], stack=stB),
            dtw=self.sb(f"dtws{half}", [64, 128], stack=stB),
            trow=self.sb(f"trows{half}", [1, 64], stack=stB),
            erow=self.sb(f"erows{half}", [1, 64], stack=stB),
        )
        for c in range(self.nch_h):
            gc = half * self.nch_h + c
            self._s3_chunk(gc, bufs)
            if getattr(self, "_stopped", False):
                return

    def _s3_chunk(self, gc, B):
        S, C, io, PS = self.sem, self.cnt, self.io, self.PS
        q = self.q
        t0 = gc * q
        Pold = self.Pst[gc % 2]
        Pnew = self.Pst[(gc + 1) % 2]
        xbp, cvt, xconv = B["xbp"], B["cvt"], B["xconv"]
        tmpc = B["tmpc"]
        Xt, BT, csT, dtT, dtwT = B["Xt"], B["BT"], B["csT"], B["dtT"], B["dtwT"]
        Gm, Esp, EspX, MT2 = B["Gm"], B["Esp"], B["EspX"], B["MT2"]
        Xp, Xw, XDp = B["Xp"], B["Xw"], B["XDp"]
        Ysb, csb4 = B["Ysb"], B["csb4"]
        adt, wch, wche, dtw = B["adt"], B["wch"], B["wche"], B["dtw"]
        trow, erow = B["trow"], B["erow"]

        pieces = [(0, 9), (9, 9), (18, 9), (27, 7)]
        vsA, ssA = C["vs"], C["ss"]
        ptoks = [[] for _ in pieces]

        def syA(e):
            for pi, (j0, jn) in enumerate(pieces):
                if pi >= 2:
                    e.wait_ge(S["vs"], vsA + pi - 1)
                for j in range(jn):
                    ptoks[pi].append(self.dq(
                        e, "L", xbp[pi % 2][:, j * 131:(j + 1) * 131],
                        io["xbc_sp"][j0 + j, :, t0:t0 + 131]))

        def veA(e):
            for pi, (j0, jn) in enumerate(pieces):
                for t in ptoks[pi]:
                    self.wt(e, t)
                if pi >= 1:
                    e.wait_ge(S["ss"], ssA + pi)
                src = xbp[pi % 2]
                v3 = src[:, 0:jn * 131].rearrange("p (j w) -> p j w", w=131)
                if gc == 0:
                    e.tensor_scalar_mul(v3[:, :, 0:3], v3[:, :, 0:3],
                                        self.halocol[:, 0:1])
                    self.vgap(e, 2)
                acc = cvt[:, 0:jn * 128].rearrange("p (j l) -> p j l", l=128)
                tmp = tmpc[:, 0:jn * 128].rearrange("p (j l) -> p j l", l=128)
                for k in range(4):
                    wb = self.cwT2[:, k * 34 + j0:k * 34 + j0 + jn] \
                        .unsqueeze(2).broadcast_to([128, jn, 128])
                    if k == 0:
                        e.tensor_tensor(acc, v3[:, :, 0:128], wb, ALU.mult)
                    else:
                        e.tensor_tensor(tmp, v3[:, :, k:k + 128], wb, ALU.mult)
                        e.tensor_tensor(acc, acc, tmp, ALU.add)
                bb = self.convbT[:, j0:j0 + jn] \
                    .unsqueeze(2).broadcast_to([128, jn, 128])
                e.tensor_tensor(acc, acc, bb, ALU.add)
                self.seminc(e, "vs")

        def scA(e):
            for pi, (j0, jn) in enumerate(pieces):
                e.wait_ge(S["vs"], vsA + pi + 1)
                a = e.activation(out=xconv[:, j0 * 128:(j0 + jn) * 128],
                                 in_=cvt[:, 0:jn * 128], func=AF.Silu)
                self.inc("ss", a)

        self.block(sync=syA, vector=veA, scalar=scA)
        if self.cfg.get("stop_after") == f"c{gc}a":
            self._stopped = True
            return

        vsB, ssB = C["vs"], C["ss"]

        def veB(e):
            dsl = self.dt_sb[0:64, t0:t0 + q]
            cssl = self.cs_sb[0:64, t0:t0 + q]
            e.tensor_scalar_mul(adt[:, :], dsl, self.acoln[:, 0:1])
            e.tensor_tensor_scan(cssl, adt[:, :], adt[:, :],
                                 0.0, ALU.add, ALU.bypass)
            self.vgap(e, 2)
            e.tensor_copy(self.tot_sb[0:64, gc:gc + 1], cssl[:, q - 1:q])
            self.vgap(e, 2)
            e.tensor_scalar_sub(wch[:, :], cssl, self.tot_sb[0:64, gc:gc + 1])
            e.tensor_add(self.ptot_sb[0:64, gc + 1:gc + 2],
                         self.ptot_sb[0:64, gc:gc + 1], self.tot_sb[0:64, gc:gc + 1])
            self.seminc(e, "vs")
            e.wait_ge(S["ss"], ssB + 1)
            e.tensor_mul(dtw[:, :], self.dt_sb[0:64, t0:t0 + q], wche[:, :])
            self.seminc(e, "vs")

        def scB(e):
            e.wait_ge(S["vs"], vsB + 1)
            a = e.activation(out=wche[:, :], in_=wch[:, :], func=AF.Exp, scale=-1.0)
            self.inc("ss", a)

        self.block(vector=veB, scalar=scB)
        if self.cfg.get("stop_after") == f"c{gc}b":
            self._stopped = True
            return

        tsC, vsC, ssC = C["ts"], C["vs"], C["ss"]

        def tnC(e):
            i = 0
            for xi in range(32):
                if i >= 8:
                    e.wait_ge(S["vs"], vsC + (i // 4) - 1)
                mm = e.transpose(PS[5 + (i // 4) % 2][:, (i % 4) * 128:(i % 4 + 1) * 128],
                                 xconv[:, xi * 128:(xi + 1) * 128], self.ident[:, :])
                self.inc("ts", mm)
                i += 1
            mm = e.transpose(PS[4][:, 0:128], xconv[:, 32 * 128:33 * 128],
                             self.ident[:, :])
            self.inc("ts", mm)
            mm = e.transpose(PS[4][:, 128:192], self.cs_sb[0:64, t0:t0 + q],
                             self.ident[0:64, 0:64])
            self.inc("ts", mm)
            mm = e.transpose(PS[4][:, 192:256], self.dt_sb[0:64, t0:t0 + q],
                             self.ident[0:64, 0:64])
            self.inc("ts", mm)
            mm = e.transpose(PS[4][:, 256:320], dtw[:, :], self.ident[0:64, 0:64])
            self.inc("ts", mm)
            mm = e.transpose(PS[7][0:1, 0:64], self.tot_sb[0:64, gc:gc + 1],
                             self.ident[0:64, 0:64])
            self.inc("ts", mm)
            mm = e.matmul(PS[4][0:128, 320:448], xconv[:, 32 * 128:33 * 128],
                          xconv[:, 33 * 128:34 * 128], start=True, stop=True)
            self.inc("ts", mm)

        def veC(e):
            for b in range(8):
                e.wait_ge(S["ts"], tsC + (b + 1) * 4)
                e.tensor_copy(Xt[:, b * 512:(b + 1) * 512], PS[5 + b % 2][:, 0:512])
                self.seminc(e, "vs")
            e.wait_ge(S["ts"], tsC + 32 + 6)
            e.tensor_copy(BT[:, :], PS[4][:, 0:128])
            e.tensor_copy(csT[:, :], PS[4][:, 128:192])
            e.tensor_copy(dtT[:, :], PS[4][:, 192:256])
            e.tensor_copy(dtwT[:, :], PS[4][:, 256:320])
            e.tensor_copy(trow[0:1, :], PS[7][0:1, 0:64])
            self.seminc(e, "vs")
            e.wait_ge(S["ts"], tsC + 32 + 6)
            e.tensor_tensor(Gm[:, :], PS[4][:, 320:448], self.maskUT[:, :], ALU.mult)
            e.tensor_copy(self.C_all[:, t0:t0 + q], xconv[:, 33 * 128:34 * 128])
            self.seminc(e, "vs")

        def scC(e):
            e.wait_ge(S["vs"], vsC + 9)
            a = e.activation(out=erow[0:1, :], in_=trow[0:1, :], func=AF.Exp)
            self.inc("ss", a)

        def gpC(e):
            e.wait_ge(S["ss"], ssC + 1)
            self.bc(e, self.AHC[:, :], erow[0:1, :], 64)
            self.dq(e, "G", io["p_sp"][gc], Pold[:, :])

        self.block(tensor=tnC, vector=veC, scalar=scC, gpsimd=gpC)
        if self.cfg.get("stop_after") == f"c{gc}c":
            self._stopped = True
            return

        vsD, ssD, tsD = C["vs"], C["ss"], C["ts"]
        cstoks = [None] * 4

        def v_cnt_xpxw(g):
            return 3 * (g + 1) + 2 * (g // 2)

        def v_cnt_bank(b):
            return 8 * b + 8

        def v_cnt_submin(g):
            return 3 * g + 1 + 2 * (g // 2)

        def gpD(e):
            for b4 in range(4):
                if b4 >= 2:
                    gl = (b4 - 2) * 4 + 3
                    e.wait_ge(S["vs"], vsD + v_cnt_submin(gl))
                cstoks[b4] = self.bc(e, csb4[b4 % 2][:, :],
                                     self.cs_sb[b4 * 16:(b4 + 1) * 16, t0:t0 + q],
                                     2048, flat_parts=16)

        def scD2(e):
            for g in range(16):
                e.wait_ge(S["vs"], vsD + v_cnt_submin(g))
                a = e.activation(out=EspX[:, :], in_=Esp[:, :], func=AF.Exp)
                self.inc("ss", a)

        def veD(e):
            for g in range(16):
                b4 = g // 4
                self.wt(e, cstoks[b4])
                sl = csb4[b4 % 2][:, (g % 4) * 512:(g % 4 + 1) * 512]
                t2 = csT[:, g * 4:g * 4 + 4].unsqueeze(2).broadcast_to([128, 4, 128])
                e.tensor_tensor(Esp[:, :].rearrange("p (h l) -> p h l", l=128),
                                sl.rearrange("p (h l) -> p h l", l=128), t2,
                                ALU.subtract)
                e.tensor_scalar_min(Esp[:, :], Esp[:, :], 0.0)
                self.seminc(e, "vs")
                e.wait_ge(S["ss"], ssD + g + 1)
                if g >= 1:
                    e.wait_ge(S["ts"], tsD + g)
                gmb = Gm[:, :].unsqueeze(1).broadcast_to([128, 4, 128])
                e.tensor_tensor(MT2[:, :].rearrange("p (h l) -> p h l", l=128),
                                EspX[:, :].rearrange("p (h l) -> p h l", l=128),
                                gmb, ALU.mult)
                self.seminc(e, "vs")
                xsl = Xt[:, g * 256:(g + 1) * 256]
                dtb = dtT[:, g * 4:g * 4 + 4].unsqueeze(2).broadcast_to([128, 4, 64])
                dwb = dtwT[:, g * 4:g * 4 + 4].unsqueeze(2).broadcast_to([128, 4, 64])
                e.tensor_tensor(Xp[g % 2][:, :].rearrange("p (h d) -> p h d", d=64),
                                xsl.rearrange("p (h d) -> p h d", d=64), dtb, ALU.mult)
                e.tensor_tensor(Xw[g % 2][:, :].rearrange("p (h d) -> p h d", d=64),
                                xsl.rearrange("p (h d) -> p h d", d=64), dwb, ALU.mult)
                self.seminc(e, "vs")
                if g % 2 == 1:
                    b = g // 2
                    e.wait_ge(S["ts"], tsD + g + 1)
                    e.tensor_tensor(
                        XDp[:, :].rearrange("p (j l) -> p j l", l=128),
                        xconv[:, b * 512:(b + 1) * 512]
                            .rearrange("p (j l) -> p j l", l=128),
                        self.dxmap[:, b * 4:(b + 1) * 4].unsqueeze(2)
                            .broadcast_to([128, 4, 128]),
                        ALU.mult)
                    e.tensor_tensor(Ysb[:, b * 512:(b + 1) * 512],
                                    PS[b % 2][:, 0:512], XDp[:, :], ALU.add)
                    self.seminc(e, "vs")
                    ab = self.AHC[:, b * 8:(b + 1) * 8].unsqueeze(2).broadcast_to([128, 8, 64])
                    e.tensor_tensor(Pnew[:, b * 512:(b + 1) * 512]
                                    .rearrange("p (h d) -> p h d", d=64),
                                    Pold[:, b * 512:(b + 1) * 512]
                                    .rearrange("p (h d) -> p h d", d=64),
                                    ab, ALU.mult)
                    e.tensor_tensor(Pnew[:, b * 512:(b + 1) * 512],
                                    Pnew[:, b * 512:(b + 1) * 512],
                                    PS[2 + b % 2][:, 0:512], ALU.add)
                    self.seminc(e, "vs")

        def tnD(e):
            for g in range(16):
                e.wait_ge(S["vs"], vsD + v_cnt_xpxw(g))
                yb = PS[(g // 2) % 2]
                sb_ = PS[2 + (g // 2) % 2]
                if g >= 4:
                    e.wait_ge(S["vs"], vsD + v_cnt_bank(g // 2 - 2))
                for hh in range(4):
                    h = g * 4 + hh
                    po = (h % 2) * 64
                    fo = ((h // 2) % 4) * 128
                    e.matmul(yb[po:po + 64, fo:fo + 128],
                             Xp[g % 2][:, hh * 64:(hh + 1) * 64],
                             MT2[:, hh * 128:(hh + 1) * 128],
                             start=True, stop=True)
                    mm = e.matmul(sb_[0:128, (h % 8) * 64:(h % 8 + 1) * 64],
                                  BT[:, :], Xw[g % 2][:, hh * 64:(hh + 1) * 64],
                                  start=True, stop=True)
                self.inc("ts", mm)

        self.block(gpsimd=gpD, vector=veD, scalar=scD2, tensor=tnD)
        if self.cfg.get("stop_after") == f"c{gc}d":
            self._stopped = True
            return

        def gpE(e):
            self.dq(e, "G", io["y_sp"][gc], Ysb[:, :])

        self.block(gpsimd=gpE)

    def m_exchange(self):
        S, C, io, PS = self.sem, self.cnt, self.io, self.PS
        cps, NCH = self.cps, self.NCH
        Pfin = self.Pst[NCH % 2]
        st = ExitStack()
        SG = self.sb("SG", [128, 4096], stack=st)
        dcol4 = self.sb("dcol4", [64, cps], stack=st)
        tree = [self.sb("F16", [64, cps * cps], stack=st)]
        w = cps * cps
        while w > cps:
            w //= 2
            tree.append(self.sb(f"Ptr{w}", [64, w], stack=st))
        coef4 = self.sb("coef4", [64, cps], stack=st)
        CJ = self.sb("CJ", [128, 64], stack=st)

        def sc(e):
            a = e.activation(out=self.segd[:, :], in_=self.ptot_sb[0:64, NCH:NCH + 1],
                             func=AF.Exp)
            self.inc("ss", a)

        self.block(scalar=sc)

        def gp(e):
            t1 = self.dq(e, "G", io["cc_in"][:, 0:4096], Pfin[:, :])
            t0 = self.dq(e, "G", io["cc_in"][:, 4096:4104], self.zpad[:, :],
                         allow_slow_non_contiguous=True)
            self.wt(e, t0)
            t2 = self.dq(e, "G", io["cc_in"][0:64, 4096:4097], self.segd[:, :],
                         allow_slow_non_contiguous=True)
            self.wt(e, t1)
            self.wt(e, t2)
            groups = [[g * cps + i for i in range(cps)]
                      for g in range(self.cfg["n_cores"] // cps)]
            cc = e.collective_compute("AllGather", ALU.bypass,
                                      replica_groups=groups,
                                      ins=[io["cc_in"][:, :]],
                                      outs=[io["cc_out"][:, :]])
            self.inc("cc", cc)

        self.block(gpsimd=gp)

        dtoks = []

        def gpd(e):
            e.wait_ge(S["cc"], 1)
            for m in range(cps):
                dtoks.append(self.dq(
                    e, "G", dcol4[:, m:m + 1],
                    io["cc_out"][m * 128:m * 128 + 64, 4096:4097],
                    allow_slow_non_contiguous=True))

        def vcf(e):
            for t in dtoks:
                self.wt(e, t)
            F16 = tree[0]
            db = dcol4[:, :].unsqueeze(1).broadcast_to([64, cps, cps])
            F3 = F16[:, :].rearrange("p (j m) -> p j m", m=cps)
            e.tensor_scalar_add(F3, db, -1.0)
            self.vgap(e, 2)
            e.tensor_mul(F16[:, :], F16[:, :], self.selDc[:, :])
            self.vgap(e, 2)
            e.tensor_scalar_add(F16[:, :], F16[:, :], 1.0)
            self.vgap(e, 2)
            for li in range(1, len(tree)):
                prev, nxt = tree[li - 1], tree[li]
                pp = prev[:, :].rearrange("p (a b) -> p a b", b=2)
                e.tensor_tensor(nxt[:, :].rearrange("p (a b) -> p a b", b=1),
                                pp[:, :, 0:1], pp[:, :, 1:2], ALU.mult)
                self.vgap(e, 2)
            e.tensor_mul(coef4[:, :], tree[-1][:, :], self.selJc[:, :])
            self.seminc(e, "vs")

        self.block(gpsimd=gpd, vector=vcf)

        def tj(e):
            mm = e.transpose(PS[7][0:cps, 0:64], coef4[:, :],
                             self.ident[0:64, 0:64])
            self.inc("ts", mm)

        self.block(tensor=tj)

        def vj(e):
            e.tensor_copy(self.rs[0:cps, 0:64], PS[7][0:cps, 0:64])
            self.seminc(e, "vs")

        self.block(vector=vj)

        for j in range(cps):
            def gpj2(e, j=j):
                self.dq(e, "G", SG[:, :],
                        io["cc_out"][j * 128:(j + 1) * 128, 0:4096])
                self.bc(e, CJ[:, :], self.rs[j:j + 1, 0:64], 64)

            self.block(gpsimd=gpj2)

            def vacc(e):
                cjb = CJ[:, :].unsqueeze(2).broadcast_to([128, 64, 64])
                e.tensor_tensor(SG[:, :].rearrange("p (h d) -> p h d", d=64),
                                SG[:, :].rearrange("p (h d) -> p h d", d=64), cjb,
                                ALU.mult)
                e.tensor_add(self.Msb[:, :], self.Msb[:, :], SG[:, :])
                self.seminc(e, "vs")

            self.block(vector=vacc)
        st.close()

    def m_s4(self, half, stC):
        q = self.q
        bufs = dict(
            Ysb4=self.sb(f"Ysb4{half}", [128, 4096], stack=stC),
            Zc=self.sb(f"Zc4{half}", [128, 4096], stack=stC),
            Scr=self.sb(f"Scr4{half}", [128, 4096], stack=stC),
            eb4=[self.sb(f"eb44{half}_{i}", [128, 2048], stack=stC) for i in range(2)],
            CeS=[self.sb(f"CeS4{half}_{i}", [128, 512], stack=stC) for i in range(2)],
            e2=self.sb(f"e24{half}", [64, 128], stack=stC),
            RST=self.sb(f"RST4{half}", [128, 128], stack=stC),
            rrow=self.sb(f"rrow4{half}", [1, 128], stack=stC),
            rrow2=self.sb(f"rrow24{half}", [1, 128], stack=stC),
            grow=self.sb(f"grow4{half}", [1, 64], stack=stC),
            GB=self.sb(f"GB4{half}", [128, 64], stack=stC),
        )
        for c in range(self.nch_h):
            gc = half * self.nch_h + c
            self._s4_chunk(gc, bufs)

    def _s4_chunk(self, gc, B):
        S, C, io, PS = self.sem, self.cnt, self.io, self.PS
        t0 = gc * self.q
        Ysb4, Zc, Scr = B["Ysb4"], B["Zc"], B["Scr"]
        eb4, CeS, e2, RST = B["eb4"], B["CeS"], B["e2"], B["RST"]
        rrow, rrow2, grow, GB = B["rrow"], B["rrow2"], B["grow"], B["GB"]
        Pr, tmpP = self.Pst[0], self.Pst[1]

        ssA, tsA, vsA = C["ss"], C["ts"], C["vs"]

        def syA(e):
            self.dq(e, "L", Pr[:, :], io["p_sp"][gc])
            self.dq(e, "L", Ysb4[:, :], io["y_sp"][gc])
            for j in range(32):
                self.dq(e, "L", Zc[:, j * 128:(j + 1) * 128],
                        io["zn_sp"][j, :, t0:t0 + 128])

        def scA(e):
            a = e.activation(out=self.cc2[:, :], in_=self.ptot_sb[0:64, gc:gc + 1],
                             func=AF.Exp)
            self.inc("ss", a)
            a = e.activation(out=e2[:, :], in_=self.cs_sb[0:64, t0:t0 + 128],
                             func=AF.Exp)
            self.inc("ss", a)

        def tnA(e):
            e.wait_ge(S["ss"], ssA + 1)
            mm = e.transpose(PS[7][0:1, 0:64], self.cc2[:, :], self.ident[0:64, 0:64])
            self.inc("ts", mm)

        def veA(e):
            e.wait_ge(S["ts"], tsA + 1)
            e.tensor_copy(grow[0:1, :], PS[7][0:1, 0:64])
            self.seminc(e, "vs")

        def gpA(e):
            e.wait_ge(S["vs"], vsA + 1)
            self.bc(e, GB[:, :], grow[0:1, :], 64)
            e.wait_ge(S["ss"], ssA + 2)
            for b4 in range(2):
                self.bc(e, eb4[b4][:, :], e2[b4 * 16:(b4 + 1) * 16, :], 2048,
                        flat_parts=16)

        self.block(sync=syA, scalar=scA, tensor=tnA, vector=veA, gpsimd=gpA)

        def veB(e):
            gb = GB[:, :].unsqueeze(2).broadcast_to([128, 64, 64])
            e.tensor_tensor(tmpP[:, :].rearrange("p (h d) -> p h d", d=64),
                            self.Msb[:, :].rearrange("p (h d) -> p h d", d=64),
                            gb, ALU.mult)
            e.tensor_add(Pr[:, :], Pr[:, :], tmpP[:, :])
            self.seminc(e, "vs")

        self.block(vector=veB)

        vsC, tsC = C["vs"], C["ts"]
        ebtoks = [None] * 4

        def v_cnt_ce(g):
            return (g + 1) + (g // 2)

        def gpC(e):
            for b4 in (2, 3):
                gl = (b4 - 2) * 4 + 3
                e.wait_ge(S["vs"], vsC + v_cnt_ce(gl))
                ebtoks[b4] = self.bc(e, eb4[b4 % 2][:, :],
                                     e2[b4 * 16:(b4 + 1) * 16, :], 2048,
                                     flat_parts=16)

        def veC(e):
            for g in range(16):
                b4 = g // 4
                if b4 >= 2:
                    self.wt(e, ebtoks[b4])
                if g >= 2:
                    e.wait_ge(S["ts"], tsC + (g - 1))
                sl = eb4[b4 % 2][:, (g % 4) * 512:(g % 4 + 1) * 512]
                cb = self.C_all[:, t0:t0 + 128].unsqueeze(1).broadcast_to([128, 4, 128])
                e.tensor_tensor(CeS[g % 2][:, :].rearrange("p (h l) -> p h l", l=128),
                                cb, sl.rearrange("p (h l) -> p h l", l=128), ALU.mult)
                self.seminc(e, "vs")
                if g % 2 == 1:
                    b = g // 2
                    e.wait_ge(S["ts"], tsC + (g + 1))
                    e.tensor_add(Ysb4[:, b * 512:(b + 1) * 512],
                                 Ysb4[:, b * 512:(b + 1) * 512], PS[b % 2][:, 0:512])
                    self.seminc(e, "vs")

        def tnC(e):
            for g in range(16):
                e.wait_ge(S["vs"], vsC + v_cnt_ce(g))
                yb = PS[(g // 2) % 2]
                if g >= 4:
                    b_prev = g // 2 - 2
                    e.wait_ge(S["vs"], vsC + 3 * b_prev + 3)
                for hh in range(4):
                    h = g * 4 + hh
                    po = (h % 2) * 64
                    fo = ((h // 2) % 4) * 128
                    mm = e.matmul(yb[po:po + 64, fo:fo + 128],
                                  Pr[:, h * 64:(h + 1) * 64],
                                  CeS[g % 2][:, hh * 128:(hh + 1) * 128],
                                  start=True, stop=True)
                self.inc("ts", mm)

        self.block(gpsimd=gpC, vector=veC, tensor=tnC)

        ssD, tsD, vsD = C["ss"], C["ts"], C["vs"]
        rtok = [None]

        def scD(e):
            a = e.activation(out=Scr[:, :], in_=Ysb4[:, :], func=AF.Square)
            self.inc("ss", a)
            e.wait_ge(S["ts"], tsD + 1)
            a = e.activation(out=rrow[0:1, :], in_=PS[7][0:1, 0:128], func=AF.Sqrt,
                             bias=self.epsc[0:1, 0:1], scale=1.0 / 4096.0)
            self.inc("ss", a)

        def tnD(e):
            e.wait_ge(S["ss"], ssD + 1)
            for i in range(32):
                mm = e.matmul(PS[7][0:1, 0:128], self.onesc[:, :],
                              Scr[:, i * 128:(i + 1) * 128],
                              start=(i == 0), stop=(i == 31))
            self.inc("ts", mm)

        def gpD(e):
            e.wait_ge(S["vs"], vsD + 1)
            rtok[0] = self.bc(e, RST[:, :], rrow2[0:1, :], 128)

        def veD(e):
            e.wait_ge(S["ss"], ssD + 2)
            e.reciprocal(rrow2[0:1, :], rrow[0:1, :])
            self.seminc(e, "vs")
            self.wt(e, rtok[0])
            rb = RST[:, :].unsqueeze(1).broadcast_to([128, 32, 128])
            e.tensor_tensor(Ysb4[:, :].rearrange("p (j l) -> p j l", l=128),
                            Ysb4[:, :].rearrange("p (j l) -> p j l", l=128), rb,
                            ALU.mult)
            e.tensor_mul(Ysb4[:, :], Ysb4[:, :], Zc[:, :])
            self.seminc(e, "vs")

        self.block(scalar=scD, tensor=tnD, gpsimd=gpD, vector=veD)

        def gpE(e):
            for j in range(32):
                self.dq(e, "G", io["gated_sp"][j, :, t0:t0 + 128],
                        Ysb4[:, j * 128:(j + 1) * 128])

        self.block(gpsimd=gpE)

    def m_s5(self, half, stD):
        S, C, io, PS = self.sem, self.cnt, self.io, self.PS
        Hc, nth = self.Hc, self.nth
        hoff = half * Hc
        NTC = Hc // 512 if Hc >= 512 else 1
        tc_n = min(512, Hc)

        acc = [self.sb(f"acc{half}_{i}", [128, Hc], stack=stD) for i in range(16)]
        gsl = [self.sb(f"gsl{half}_{i}", [128, tc_n], stack=stD) for i in range(8)]
        wsl2 = [self.sb(f"wsl2{half}_{i}", [128, 1024], stack=stD) for i in range(2)]

        for kg in range(4):
            for tci in range(NTC):
                tc0 = tci * tc_n
                ts1, vs1 = C["ts"], C["vs"]
                gtoks, stoks = [], []

                def syK(e, kg=kg, tc0=tc0, ts1=ts1, gtoks=gtoks, stoks=stoks):
                    for i in range(8):
                        gtoks.append(self.dq(
                            e, "L", gsl[i][:, 0:tc_n],
                            io["gated_sp"][kg * 8 + i, :,
                                           hoff + tc0:hoff + tc0 + tc_n]))
                    for mt in range(16):
                        if mt >= 2:
                            e.wait_ge(S["ts"], ts1 + mt - 1)
                        stoks.append(self.dq(
                            e, "L", wsl2[mt % 2][:, :],
                            io["wopT_t"][mt, kg].rearrange("kr kt mc -> kr (kt mc)")))

                def tnK(e, kg=kg, tc0=tc0, vs1=vs1, gtoks=gtoks, stoks=stoks):
                    for t in gtoks:
                        self.wt(e, t)
                    for mt in range(16):
                        self.wt(e, stoks[mt])
                        pb = PS[mt % 4]
                        if mt >= 4:
                            e.wait_ge(S["vs"], vs1 + mt - 3)
                        for kt in range(8):
                            mm = e.matmul(pb[0:128, 0:tc_n],
                                          wsl2[mt % 2][:, kt * 128:(kt + 1) * 128],
                                          gsl[kt][:, 0:tc_n],
                                          start=(kt == 0), stop=(kt == 7))
                        self.inc("ts", mm)

                def veK(e, kg=kg, tc0=tc0, ts1=ts1):
                    for mt in range(16):
                        e.wait_ge(S["ts"], ts1 + mt + 1)
                        if kg == 0:
                            e.tensor_copy(acc[mt][:, tc0:tc0 + tc_n],
                                          PS[mt % 4][0:128, 0:tc_n])
                        else:
                            e.tensor_add(acc[mt][:, tc0:tc0 + tc_n],
                                         acc[mt][:, tc0:tc0 + tc_n],
                                         PS[mt % 4][0:128, 0:tc_n])
                        self.seminc(e, "vs")

                self.block(sync=syK, tensor=tnK, vector=veK)

        xt5 = self.sb(f"xt5{half}", [128, HD], stack=stD)
        x1t = self.sb(f"x1t{half}", [128, 128], stack=stD)
        sqt = [self.sb(f"sqt{half}_{i}", [128, 128], stack=stD) for i in range(2)]
        murow, sqrow = self.murow, self.sqrow

        for tt in range(nth):
            r0 = 3 + hoff + tt * 128
            ts0, vs0, ss0 = C["ts"], C["vs"], C["ss"]
            xtk = [None]
            shtoks = []

            def syX(e, r0=r0):
                xtk[0] = self.dq(e, "L", xt5[:, :], io["x_sl"][r0:r0 + 128, :])

            def tnX(e, vs0=vs0, ss0=ss0):
                self.wt(e, xtk[0])
                for ft in range(16):
                    if ft >= 2:
                        e.wait_ge(S["vs"], vs0 + ft - 1)
                    mm = e.transpose(PS[4 + ft % 2][:, 0:128],
                                     xt5[:, ft * 128:(ft + 1) * 128], self.ident[:, :])
                    self.inc("ts", mm)
                    e.wait_ge(S["vs"], vs0 + ft + 1)
                    mm = e.matmul(PS[6][0:1, 0:128], self.onesc[:, :], x1t[:, :],
                                  start=(ft == 0), stop=(ft == 15),
                                  skip_group_check=True)
                    self.inc("ts", mm)
                    e.wait_ge(S["ss"], ss0 + ft + 1)
                    mm = e.matmul(PS[7][0:1, 0:128], self.onesc[:, :],
                                  sqt[ft % 2][:, :],
                                  start=(ft == 0), stop=(ft == 15),
                                  skip_group_check=True)
                    self.inc("ts", mm)

            def veX(e, tt=tt, ts0=ts0, ss0=ss0):
                for ft in range(16):
                    e.wait_ge(S["ts"], ts0 + 3 * ft + 1)
                    if ft >= 1:
                        self.wt(e, shtoks[ft - 1])
                        e.wait_ge(S["ts"], ts0 + 3 * ft)
                        e.wait_ge(S["ss"], ss0 + ft)
                    e.scalar_tensor_tensor(x1t[:, :],
                                           acc[ft][:, tt * 128:(tt + 1) * 128],
                                           self.gaC[:, ft:ft + 1],
                                           PS[4 + ft % 2][:, 0:128],
                                           ALU.mult, ALU.add)
                    self.seminc(e, "vs")

            def scX(e, ts0=ts0, vs0=vs0):
                for ft in range(16):
                    e.wait_ge(S["vs"], vs0 + ft + 1)
                    if ft >= 2:
                        e.wait_ge(S["ts"], ts0 + 3 * (ft - 1))
                    a = e.activation(out=sqt[ft % 2][:, :], in_=x1t[:, :],
                                     func=AF.Square)
                    self.inc("ss", a)

            def gpX(e, tt=tt, vs0=vs0):
                for ft in range(16):
                    e.wait_ge(S["vs"], vs0 + ft + 1)
                    shtoks.append(self.dq(
                        e, "L" if False else "G",
                        io["x1_sp"][ft, :, hoff + tt * 128:hoff + (tt + 1) * 128],
                        x1t[:, :]))

            def veX2(e, tt=tt):
                e.tensor_copy(murow[0:1, tt * 128:(tt + 1) * 128], PS[6][0:1, 0:128])
                e.tensor_copy(sqrow[0:1, tt * 128:(tt + 1) * 128], PS[7][0:1, 0:128])
                self.seminc(e, "vs")

            self.block(sync=syX, tensor=tnX, scalar=scX, gpsimd=gpX, vector=veX)
            self.block(vector=veX2)

        rs_row = self.sb(f"rs_row{half}", [1, Hc], stack=stD)
        vsR, ssR = C["vs"], C["ss"]

        def veR(e):
            e.tensor_scalar_mul(murow[0:1, :], murow[0:1, :], 1.0 / HD)
            self.seminc(e, "vs")
            e.wait_ge(S["ss"], ssR + 1)
            e.scalar_tensor_tensor(sqrow[0:1, :], sqrow[0:1, :], 1.0 / HD,
                                   rs_row[0:1, :], ALU.mult, ALU.subtract)
            self.seminc(e, "vs")
            e.wait_ge(S["ss"], ssR + 2)
            e.reciprocal(sqrow[0:1, :], rs_row[0:1, :])
            self.seminc(e, "vs")

        def scR(e):
            e.wait_ge(S["vs"], vsR + 1)
            a = e.activation(out=rs_row[0:1, :], in_=murow[0:1, :],
                             func=AF.Square)
            self.inc("ss", a)
            e.wait_ge(S["vs"], vsR + 2)
            a = e.activation(out=rs_row[0:1, :], in_=sqrow[0:1, :],
                             func=AF.Sqrt, bias=self.epsc[0:1, 0:1], scale=1.0)
            self.inc("ss", a)

        self.block(vector=veR, scalar=scR)

    def m_s6(self, half, stE):
        S, C, io, PS = self.sem, self.cnt, self.io, self.PS
        Hc, nth = self.Hc, self.nth
        hoff = half * Hc
        murow, rstdrow = self.murow, self.sqrow

        h2 = [self.sb(f"h2{half}_{i}", [128, Hc], stack=stE) for i in range(16)]
        x1r = self.sb(f"x1r{half}", [128, Hc], stack=stE)
        MUB = self.sb(f"MUB{half}", [128, Hc], stack=stE)
        RSB = self.sb(f"RSB{half}", [128, Hc], stack=stE)
        wsl6 = [self.sb(f"wsl6{half}_{i}", [128, HD], stack=stE) for i in range(2)]
        gt = self.sb(f"gt{half}", [128, 512], stack=stE)
        hht = [self.sb(f"hht{half}_{i}", [128, 512], stack=stE) for i in range(2)]
        mb = self.sb(f"mb{half}", [128, 512], stack=stE)

        def gpB(e):
            self.bc(e, MUB[:, :], murow[0:1, :], Hc)
            self.bc(e, RSB[:, :], rstdrow[0:1, :], Hc)

        self.block(gpsimd=gpB)

        for ft in range(16):
            htk = [None]

            def syH(e, ft=ft):
                htk[0] = self.dq(e, "L", x1r[:, :],
                                 io["x1_sp"][ft, :, hoff:hoff + Hc])

            def veH(e, ft=ft):
                self.wt(e, htk[0])
                e.tensor_tensor(h2[ft][:, :], x1r[:, :], MUB[:, :], ALU.subtract)
                e.tensor_mul(h2[ft][:, :], h2[ft][:, :], RSB[:, :])
                e.tensor_scalar(h2[ft][:, :], h2[ft][:, :], self.cb1C[:, ft:ft + 1],
                                self.sbC[:, ft:ft + 1], ALU.mult, ALU.add)
                self.seminc(e, "vs")

            self.block(sync=syH, vector=veH)

        for tt in range(nth):
            ts0, vs0 = C["ts"], C["vs"]

            def tnR(e, tt=tt):
                for kt in range(16):
                    mm = e.matmul(PS[4][0:128, 0:16], h2[kt][:, tt * 128:(tt + 1) * 128],
                                  self.wrS[:, kt * 16:(kt + 1) * 16],
                                  start=(kt == 0), stop=(kt == 15))
                self.inc("ts", mm)

            def veR(e, tt=tt, ts0=ts0):
                e.wait_ge(S["ts"], ts0 + 1)
                e.tensor_add(self.mrow[:, :], PS[4][0:128, 0:16], self.brB[:, :])
                self.vgap(e, 2)
                e.max(out=self.sm8[:, :], in_=self.mrow[:, :])
                self.vgap(e, 2)
                e.tensor_scalar(self.mrow[:, :], self.mrow[:, :],
                                self.sm8[:, 3:4], None, ALU.is_ge)
                self.seminc(e, "vs")

            def tnR2(e, tt=tt, vs0=vs0):
                e.wait_ge(S["vs"], vs0 + 1)
                mm = e.transpose(PS[4][0:16, 128:256], self.mrow[:, :], self.ident[:, :])
                self.inc("ts", mm)

            def veR2(e, tt=tt, ts0=ts0):
                e.wait_ge(S["ts"], ts0 + 2)
                e.tensor_copy(self.maskF[0:16, tt * 128:(tt + 1) * 128],
                              PS[4][0:16, 128:256])
                self.seminc(e, "vs")

            self.block(tensor=tnR, vector=veR)
            self.block(tensor=tnR2, vector=veR2)

        NTC = max(1, Hc // 512)
        tc_n = min(512, Hc)
        mbb = [self.sb(f"mbb{half}_{i}", [128, tc_n], stack=stE) for i in range(NTC)]
        for mt in range(64):
            b = mt // 4
            ts0, vs0, ss0 = C["ts"], C["vs"], C["ss"]
            wtk = [None]
            mtoks = []

            def syW(e, mt=mt):
                wtk[0] = self.dq(e, "L", wsl6[mt % 2][:, :],
                                 io["w1T_t"][mt].rearrange("kr kt mc -> kr (kt mc)"))

            def gpW(e, mt=mt, b=b, vs0=vs0):
                if mt % 4 == 0:
                    for tci in range(NTC):
                        mtoks.append(self.bc(
                            e, mbb[tci][:, :],
                            self.maskF[b:b + 1, tci * tc_n:(tci + 1) * tc_n],
                            tc_n))
                for tci in range(NTC):
                    e.wait_ge(S["vs"], vs0 + tci + 1)
                    j = mt * NTC + tci
                    self.dq(e, "G",
                            io["hh_sp"][mt, :, hoff + tci * tc_n:
                                        hoff + (tci + 1) * tc_n],
                            hht[j % 2][:, 0:tc_n])

            nmb = NTC if mt % 4 == 0 else 0

            def tnW(e, mt=mt):
                self.wt(e, wtk[0])
                for tci in range(NTC):
                    for kt in range(16):
                        mm = e.matmul(PS[tci % 2][0:128, 0:tc_n],
                                      wsl6[mt % 2][:, kt * 128:(kt + 1) * 128],
                                      h2[kt][:, tci * tc_n:(tci + 1) * tc_n],
                                      start=(kt == 0), stop=(kt == 15))
                    self.inc("ts", mm)

            def scW(e, ts0=ts0, vs0=vs0):
                for tci in range(NTC):
                    e.wait_ge(S["ts"], ts0 + tci + 1)
                    if tci >= 1:
                        e.wait_ge(S["vs"], vs0 + tci)
                    a = e.activation(out=gt[:, 0:tc_n], in_=PS[tci % 2][0:128, 0:tc_n],
                                     func=AF.Gelu)
                    self.inc("ss", a)

            def veW(e, mt=mt, ss0=ss0, nmb=nmb):
                for tci in range(NTC):
                    e.wait_ge(S["ss"], ss0 + tci + 1)
                    if nmb:
                        self.wt(e, mtoks[tci])
                    j = mt * NTC + tci
                    e.tensor_mul(hht[j % 2][:, 0:tc_n], gt[:, 0:tc_n], mbb[tci][:, :])
                    self.seminc(e, "vs")

            self.block(sync=syW, gpsimd=gpW, tensor=tnW, scalar=scW, vector=veW)

    def m_s7(self, half, stF):
        S, C, io, PS = self.sem, self.cnt, self.io, self.PS
        Hc, nth = self.Hc, self.nth
        hoff = half * Hc
        NTC = Hc // 512 if Hc >= 512 else 1
        tc_n = min(512, Hc)

        acc2 = [self.sb(f"acc2{half}_{i}", [128, Hc], stack=stF) for i in range(16)]
        hsl = [self.sb(f"hsl{half}_{i}", [128, tc_n], stack=stF) for i in range(8)]
        wsl7 = [self.sb(f"wsl7{half}_{i}", [128, 1024], stack=stF) for i in range(2)]

        for kg in range(8):
            for tci in range(NTC):
                tc0 = tci * tc_n
                ts1, vs1 = C["ts"], C["vs"]
                htoks, stoks = [], []

                def syK(e, kg=kg, tc0=tc0, ts1=ts1, htoks=htoks, stoks=stoks):
                    for i in range(8):
                        htoks.append(self.dq(
                            e, "L", hsl[i][:, 0:tc_n],
                            io["hh_sp"][kg * 8 + i, :,
                                        hoff + tc0:hoff + tc0 + tc_n]))
                    for mt in range(16):
                        if mt >= 2:
                            e.wait_ge(S["ts"], ts1 + mt - 1)
                        stoks.append(self.dq(
                            e, "L", wsl7[mt % 2][:, :],
                            io["w2T_t"][mt, kg].rearrange("kr kt mc -> kr (kt mc)")))

                def tnK(e, tc0=tc0, vs1=vs1, htoks=htoks, stoks=stoks):
                    for t in htoks:
                        self.wt(e, t)
                    for mt in range(16):
                        self.wt(e, stoks[mt])
                        pb = PS[mt % 4]
                        if mt >= 4:
                            e.wait_ge(S["vs"], vs1 + mt - 3)
                        for kt in range(8):
                            mm = e.matmul(pb[0:128, 0:tc_n],
                                          wsl7[mt % 2][:, kt * 128:(kt + 1) * 128],
                                          hsl[kt][:, 0:tc_n],
                                          start=(kt == 0), stop=(kt == 7))
                        self.inc("ts", mm)

                def veK(e, kg=kg, tc0=tc0, ts1=ts1):
                    for mt in range(16):
                        e.wait_ge(S["ts"], ts1 + mt + 1)
                        if kg == 0:
                            e.tensor_copy(acc2[mt][:, tc0:tc0 + tc_n],
                                          PS[mt % 4][0:128, 0:tc_n])
                        else:
                            e.tensor_add(acc2[mt][:, tc0:tc0 + tc_n],
                                         acc2[mt][:, tc0:tc0 + tc_n],
                                         PS[mt % 4][0:128, 0:tc_n])
                        self.seminc(e, "vs")

                self.block(sync=syK, tensor=tnK, vector=veK)

        x1r = self.sb(f"x1r7{half}", [128, Hc], stack=stF)
        ot = [self.sb(f"ot{half}_{i}", [128, HD], stack=stF) for i in range(2)]

        for ft in range(16):
            ftk = [None]

            def syF(e, ft=ft):
                ftk[0] = self.dq(e, "L", x1r[:, :],
                                 io["x1_sp"][ft, :, hoff:hoff + Hc])

            def veF(e, ft=ft):
                self.wt(e, ftk[0])
                e.scalar_tensor_tensor(acc2[ft][:, :], acc2[ft][:, :],
                                       self.gbC[:, ft:ft + 1], x1r[:, :],
                                       ALU.mult, ALU.add)
                self.seminc(e, "vs")

            self.block(sync=syF, vector=veF)

        outoks = []
        for tt in range(nth):
            ts0, vs0 = C["ts"], C["vs"]

            def tnO(e, tt=tt, vs0=vs0):
                for ft in range(16):
                    if ft >= 2:
                        e.wait_ge(S["vs"], vs0 + ft - 1)
                    mm = e.transpose(PS[4 + ft % 2][:, 0:128],
                                     acc2[ft][:, tt * 128:(tt + 1) * 128],
                                     self.ident[:, :])
                    self.inc("ts", mm)

            def gpO(e, tt=tt, vs0=vs0):
                e.wait_ge(S["vs"], vs0 + 16)
                outoks.append(self.dq(
                    e, "G", io["y_out"][hoff + tt * 128:hoff + (tt + 1) * 128, :],
                    ot[tt % 2][:, :]))

            def veO(e, tt=tt, ts0=ts0):
                if tt >= 2:
                    self.wt(e, outoks[tt - 2])
                for ft in range(16):
                    e.wait_ge(S["ts"], ts0 + ft + 1)
                    e.tensor_copy(ot[tt % 2][:, ft * 128:(ft + 1) * 128],
                                  PS[4 + ft % 2][:, 0:128])
                    self.seminc(e, "vs")

            self.block(tensor=tnO, gpsimd=gpO, vector=veO)

    def m_finish(self):
        S, C = self.sem, self.cnt

        def gp(e):
            for name, v in C.items():
                if v > 0:
                    e.wait_ge(S[name], v)

        self.block(gpsimd=gp)



def _prep_shared(inputs, cfg):
    f = np.float32
    Wip = np.asarray(inputs["L0_attn_in_proj_w"], f)
    WT = np.ascontiguousarray(Wip.T)
    WTp = np.zeros((HD, N_MT * 128), f)
    WTp[:, :8512] = WT
    wipT_t = np.ascontiguousarray(
        WTp.reshape(16, 128, N_MT, 128).transpose(2, 1, 0, 3))

    Wop = np.asarray(inputs["L0_attn_out_proj_w"], f)
    WT = np.ascontiguousarray(Wop.T)
    wopT_t = np.ascontiguousarray(
        WT.reshape(4, 8, 128, 16, 128).transpose(3, 0, 2, 1, 4))

    W1 = np.asarray(inputs["L0_ffn_W1"], f)
    WT = np.ascontiguousarray(W1.T)
    w1T_t = np.ascontiguousarray(
        WT.reshape(16, 128, 64, 128).transpose(2, 1, 0, 3))

    W2 = np.asarray(inputs["L0_ffn_W2"], f)
    WT = np.ascontiguousarray(W2.T)
    w2T_t = np.ascontiguousarray(
        WT.reshape(8, 8, 128, 16, 128).transpose(3, 0, 2, 1, 4))

    Wr = np.asarray(inputs["L0_ffn_Wr"], f)
    wrT_t = np.ascontiguousarray(Wr.T.reshape(16, 128, 16).transpose(1, 0, 2))

    wmod = np.asarray(inputs["L0_wmod"], f)
    wmodT_t = np.ascontiguousarray(
        wmod.reshape(16, 128, 96, 128).transpose(2, 1, 0, 3))

    cw = np.asarray(inputs["L0_attn_conv_w"], f)[:, 0, :]
    cwT2 = np.ascontiguousarray(
        cw.reshape(34, 128, 4).transpose(1, 2, 0).reshape(128, 4 * 34))
    convbT = np.ascontiguousarray(
        np.asarray(inputs["L0_attn_conv_b"], f).reshape(34, 128).T)

    D = np.asarray(inputs["L0_attn_D"], f)
    dxmap = np.ascontiguousarray(np.repeat(D, 64).reshape(32, 128).T)
    nw = np.asarray(inputs["L0_attn_norm_w"], f)
    nwmap = np.ascontiguousarray(nw.reshape(32, 128).T)

    dtb = np.ascontiguousarray(np.asarray(inputs["L0_attn_dt_bias"], f).reshape(64, 1))
    alog = np.ascontiguousarray(np.asarray(inputs["L0_attn_A_log"], f).reshape(64, 1))
    ident = np.eye(128, dtype=f)
    epsv = np.full((128, 1), 1e-5, f)
    maskUT = np.ascontiguousarray(
        (np.arange(128)[None, :] >= np.arange(128)[:, None]).astype(f))
    br = np.ascontiguousarray(np.asarray(inputs["L0_ffn_br"], f).reshape(1, 16))
    return dict(wipT_t=wipT_t, wopT_t=wopT_t, w1T_t=w1T_t, w2T_t=w2T_t,
                wrT_t=wrT_t, wmodT_t=wmodT_t, cwT2=cwT2, convbT=convbT,
                dxmap=dxmap, nwmap=nwmap, dtb=dtb, alog=alog, ident=ident,
                maskUT=maskUT, br=br, epsv=epsv)


def make_in_maps(inputs, cfg):
    f = np.float32
    n_cores, cps, Tc = cfg["n_cores"], cfg["cps"], cfg["Tc"]
    shared = _prep_shared(inputs, cfg)
    x = np.asarray(inputs["L0_attn_hidden_states"], f)
    cmod = np.asarray(inputs["L0_c"], f)
    bsz, L, _ = x.shape
    assert cps * Tc == L and n_cores == bsz * cps

    in_maps = []
    for core in range(n_cores):
        b = core // cps
        qq = core % cps
        t0 = qq * Tc
        x_sl = np.zeros((Tc + 3, HD), f)
        if t0 >= 3:
            x_sl[:3] = x[b, t0 - 3:t0]
        x_sl[3:] = x[b, t0:t0 + Tc]
        halo = np.full((1, 1), 0.0 if qq == 0 else 1.0, f)
        c_T = np.ascontiguousarray(cmod[b].reshape(16, 128).T)
        selJ = np.zeros((64, cps), f)
        selD = np.zeros((64, cps * cps), f)
        for j in range(cps):
            if j < qq:
                selJ[:, j] = 1.0
            for m in range(cps):
                if j < m < qq:
                    selD[:, j * cps + m] = 1.0
        m = dict(shared)
        m.update(x_sl=np.ascontiguousarray(x_sl), halo=halo, c_T=c_T,
                 selJ=np.ascontiguousarray(selJ), selD=np.ascontiguousarray(selD))
        in_maps.append(m)
    return in_maps


def kernel(**inputs):
    cfg = default_cfg()
    mk = MK(cfg)
    nc = mk.build()
    in_maps = make_in_maps(inputs, cfg)
    res = run_bass_kernel_spmd(nc, in_maps, core_ids=list(range(cfg["n_cores"])))
    x = np.asarray(inputs["L0_attn_hidden_states"], np.float32)
    bsz, L, _ = x.shape
    out = np.empty_like(x)
    for core in range(cfg["n_cores"]):
        b = core // cfg["cps"]
        qq = core % cfg["cps"]
        out[b, qq * cfg["Tc"]:(qq + 1) * cfg["Tc"]] = res.results[core]["y_out"]
    return out
